# version 08 (final)
# baseline (speedup 1.0000x reference)
import numpy as np
import ml_dtypes
import concourse.bass as bass
import concourse.mybir as mybir
from contextlib import ExitStack

F32 = mybir.dt.float32
BF16 = mybir.dt.bfloat16
AF = mybir.ActivationFunctionType
ALU = mybir.AluOpType
AX = mybir.AxisListType

D = 1024
HF = 4096
NE = 8
NJC = 32
RT2 = float(1.0 / np.sqrt(128.0))


class Eng:
    def __init__(self, name, auto_self=False):
        self.name = name
        self.eng = None
        self.sem = None
        self.n = 0
        self.auto_self = auto_self

    def op(self, fn, *args, mark=None, rec=None, **kw):
        if self.auto_self and self.eng is not None and self.n > 0:
            self.eng.wait_ge(self.sem, self.n)
        self.n += 1
        if self.eng is not None:
            getattr(self.eng, fn)(*args, **kw).then_inc(self.sem, 1)
        if mark is not None:
            rec.marks[mark] = (self.name, self.n)

    def mm(self, *args, mark=None, rec=None, **kw):
        self.op("matmul", *args, mark=mark, rec=rec, **kw)

    def dma(self, dst, src, key, rec, mark=None):
        rec.dn[key] = rec.dn.get(key, 0) + 16
        if self.eng is not None:
            self.eng.dma_start(dst, src).then_inc(rec.dsems[key], 16)
        if mark is not None:
            rec.marks[mark] = ("DMA:" + key, rec.dn[key])

    def wait(self, mark, rec):
        if self.eng is None:
            return
        src, val = rec.marks[mark]
        if src.startswith("DMA:"):
            self.eng.wait_ge(rec.dsems[src[4:]], val)
        else:
            self.eng.wait_ge(rec.engsems[src], val)

    def wait_dma(self, key, val, rec):
        if self.eng is not None:
            self.eng.wait_ge(rec.dsems[key], val)


class Rec:
    def __init__(self):
        self.marks = {}
        self.dsems = {}
        self.dn = {}
        self.engs = {}
        self.engsems = {}


def host_prep(inputs, nb, ntt, core_b0):
    f32 = np.float32
    TT = ntt * 128
    d = {}
    d["x"] = np.ascontiguousarray(np.asarray(inputs["L0_attn_x"])[core_b0:core_b0 + nb, :TT]).astype(f32, copy=False)
    d["wqt"] = np.ascontiguousarray(np.asarray(inputs["L0_attn_Wq"]).T)
    d["wkt"] = np.ascontiguousarray(np.asarray(inputs["L0_attn_Wk"]).T)
    d["wvt"] = np.ascontiguousarray(np.asarray(inputs["L0_attn_Wv"]).T)
    d["wot"] = np.ascontiguousarray(np.asarray(inputs["L0_attn_Wo"]).T)
    d["wgat"] = np.ascontiguousarray(np.asarray(inputs["L0_attn_Wg"]).T)
    d["bga"] = np.ascontiguousarray(np.asarray(inputs["L0_attn_bg"], f32).reshape(1, 8))
    d["qk"] = np.ascontiguousarray(np.asarray(inputs["L0_attn_qk_gain"], f32).reshape(1, 1))
    d["wgf"] = np.ascontiguousarray(np.asarray(inputs["L0_ffn_Wg"]))
    d["bgf"] = np.ascontiguousarray(np.asarray(inputs["L0_ffn_bg"], f32).reshape(1, 8))
    d["w1p"] = np.ascontiguousarray(
        np.asarray(inputs["L0_ffn_W1"]).reshape(NE, 8, 128, NJC, 128).transpose(0, 3, 2, 1, 4).reshape(NE, NJC, 128, 1024))
    d["w2n"] = np.ascontiguousarray(np.asarray(inputs["L0_ffn_W2"]))
    d["b1c"] = np.ascontiguousarray(np.asarray(inputs["L0_ffn_b1"]).reshape(NE, NJC, 128).transpose(0, 2, 1))
    d["b2r"] = np.ascontiguousarray(np.asarray(inputs["L0_ffn_b2"]))
    t_i = np.arange(128)[:, None]
    s_i = np.arange(128)[None, :]
    blockdiag = (t_i // 64 == s_i // 64) & (s_i <= t_i)
    d["cmask"] = np.where(blockdiag, 0.0, -1e30).astype(f32)
    d["pow2"] = np.broadcast_to(2.0 ** -np.arange(8, dtype=f32), (128, 8)).copy()
    d["onesb"] = np.ones((1, 128), ml_dtypes.bfloat16)
    d["onesf"] = np.ones((1, 128), f32)
    d["identb"] = np.eye(128, dtype=f32).astype(ml_dtypes.bfloat16)
    ep = np.zeros((128, 4), f32)
    ep[:, 0] = 1e-5; ep[:, 1] = 1e-6; ep[:, 2] = 1.0
    d["epsc"] = ep
    return d


def build(nb, ntt):
    NTOT = nb * ntt
    NT = NTOT * 128
    NTB = NT // 512
    assert NT % 512 == 0
    TT = ntt * 128

    nc = bass.Bass("TRN2", target_bir_lowering=False)
    inp = {}
    for name, shape, dt in [
        ("x", [nb, TT, D], F32), ("wqt", [D, D], F32), ("wkt", [D, D], F32),
        ("wvt", [D, D], F32), ("wot", [D, D], F32), ("wgat", [D, 8], F32),
        ("bga", [1, 8], F32), ("qk", [1, 1], F32), ("wgf", [D, 8], F32),
        ("bgf", [1, 8], F32), ("w1p", [NE, NJC, 128, 1024], F32),
        ("w2n", [NE, HF, D], F32), ("b1c", [NE, 128, NJC], F32), ("b2r", [NE, D], F32),
        ("cmask", [128, 128], F32), ("pow2", [128, 8], F32),
        ("onesb", [1, 128], BF16), ("onesf", [1, 128], F32), ("identb", [128, 128], BF16), ("epsc", [128, 4], F32),
    ]:
        inp[name] = nc.dram_tensor(name, shape, dt, kind="ExternalInput").ap()
    out_d = nc.dram_tensor("out", [nb, TT, D], F32, kind="ExternalOutput").ap()
    r_d = nc.dram_tensor("r_scr", [NT, D], F32, kind="Internal").ap()
    gam_d = nc.dram_tensor("gam_scr", [NTOT, 16], F32, kind="Internal").ap()
    w1bf_d = nc.dram_tensor("w1bf", [NE, NJC, 128, 1024], BF16, kind="Internal").ap()
    w2bf_d = nc.dram_tensor("w2bf", [NE, HF, D], BF16, kind="Internal").ap()

    rec = Rec()
    T = Eng("T"); V = Eng("V", auto_self=True); S = Eng("S", auto_self=True); SY = Eng("SY"); G = Eng("G")
    rec.engs = {e.name: e for e in (T, V, S, SY, G)}

    with ExitStack() as ctx:
        En = ctx.enter_context

        xnT = En(nc.sbuf_tensor("xnT", [128, 8 * NT], BF16))
        oh = En(nc.sbuf_tensor("oh", [128, NTOT * 8], F32))
        WT_holder = []
        wgat_sb = En(nc.sbuf_tensor("wgat_sb", [128, 64], BF16))
        wgf_sb = En(nc.sbuf_tensor("wgf_sb", [128, 64], BF16))
        bga_sb = En(nc.sbuf_tensor("bga_sb", [1, 8], F32))
        bgf_sb = En(nc.sbuf_tensor("bgf_sb", [1, 8], F32))
        cmask_sb = En(nc.sbuf_tensor("cmask_sb", [128, 128], F32))
        pow2_sb = En(nc.sbuf_tensor("pow2_sb", [128, 8], F32))
        onesb_sb = En(nc.sbuf_tensor("onesb_sb", [1, 128], BF16))
        onesf_sb = En(nc.sbuf_tensor("onesf_sb", [1, 128], F32))
        ident_sb = En(nc.sbuf_tensor("ident_sb", [128, 128], BF16))
        qk_sb = En(nc.sbuf_tensor("qk_sb", [1, 1], F32))
        lnqk_sb = En(nc.sbuf_tensor("lnqk_sb", [128, 1], F32))
        b2sb = En(nc.sbuf_tensor("b2sb", [1, NE * D], BF16))
        b2all = En(nc.sbuf_tensor("b2all", [8, D], BF16))
        ohb = En(nc.sbuf_tensor("ohb", [128, 8], BF16))
        ohT_sb = En(nc.sbuf_tensor("ohT_sb", [8, 128], BF16))
        epsc_sb = En(nc.sbuf_tensor("epsc_sb", [128, 4], F32))
        b1sb = En(nc.sbuf_tensor("b1sb", [128, NE * NJC], F32))

        dkeys = (["ld", "ldg"] + ["pc%d" % e for e in range(NE)]
                 + ["xs0", "xs1", "xs2", "rst0", "rst1"]
                 + ["w1s%d" % s for s in range(8)] + ["w2s%d" % s for s in range(8)]
                 + ["fr0", "fr1", "fst0", "fst1", "gw", "gr"])
        for k in dkeys:
            rec.dsems[k] = En(nc.semaphore("ds_" + k))
        sems = {}
        for e in (T, V, S, SY, G):
            sems[e.name] = En(nc.semaphore("sem_" + e.name))
        rec.engsems = sems

        def reset(emit):
            pairs = [(T, nc.tensor), (V, nc.vector), (S, nc.scalar), (SY, nc.sync), (G, nc.gpsimd)]
            for e, real in pairs:
                e.eng = real if emit else None
                e.sem = sems[e.name] if emit else None
                e.n = 0
            rec.dn = {}

        def phase0():
            SY.dma(cmask_sb[:, :], inp["cmask"], "ld", rec)
            SY.dma(pow2_sb[:, :], inp["pow2"], "ld", rec)
            SY.dma(onesf_sb[:, :], inp["onesf"], "ld", rec)
            SY.dma(onesb_sb[:, :], inp["onesb"], "ld", rec)
            SY.dma(qk_sb[:, :], inp["qk"], "ld", rec)
            SY.dma(bga_sb[:, :], inp["bga"], "ld", rec)
            SY.dma(bgf_sb[:, :], inp["bgf"], "ld", rec)
            SY.dma(b1sb[:, :].rearrange("p (e c) -> p e c", e=NE),
                   inp["b1c"].rearrange("e p c -> p e c"), "ld", rec)
            SY.dma(epsc_sb[:, :], inp["epsc"], "ld", rec, mark="ld_done")
            for w in ("wqt", "wkt", "wvt", "wot"):
                G.dma(WT[w][:, :].rearrange("p (dc o) -> p dc o", dc=8),
                      inp[w].rearrange("(dc p) o -> p dc o", p=128), "ldg", rec)
            G.dma(wgat_sb[:, :].rearrange("p (dc h) -> p dc h", dc=8),
                  inp["wgat"].rearrange("(dc p) h -> p dc h", p=128), "ldg", rec)
            G.dma(wgf_sb[:, :].rearrange("p (dc h) -> p dc h", dc=8),
                  inp["wgf"].rearrange("(dc p) h -> p dc h", p=128), "ldg", rec)
            G.dma(b2sb[:, :].rearrange("o (e d) -> o e d", e=NE),
                  inp["b2r"].unsqueeze(0), "ldg", rec)
            G.dma(b2all[:, :], inp["b2r"], "ldg", rec)
            G.dma(ident_sb[:, :], inp["identb"], "ldg", rec, mark="ldg_done")

        pc_chunks = []
        for e in range(NE):
            for q4 in range(4):
                pc_chunks.append((w1bf_d[e, q4 * 8:(q4 + 1) * 8],
                                  inp["w1p"][e, q4 * 8:(q4 + 1) * 8], "pc%d" % e))
            for q4 in range(4):
                pc_chunks.append((w2bf_d[e, q4 * 1024:(q4 + 1) * 1024],
                                  inp["w2n"][e, q4 * 1024:(q4 + 1) * 1024], "pc%d" % e))

        attn_ctx = ExitStack()
        An = attn_ctx.enter_context
        WT = {w: An(nc.sbuf_tensor(w + "_sb", [128, 8 * 1024], BF16))
              for w in ("wqt", "wkt", "wvt", "wot")}
        x_sb = An(nc.sbuf_tensor("x_sb", [128, 3 * 1024], F32))
        junk = An(nc.sbuf_tensor("junk", [128, 1024], BF16))
        small = An(nc.sbuf_tensor("small", [128, 16], F32))
        sqq = An(nc.sbuf_tensor("sqq", [128, 1024], BF16))
        ssq = An(nc.sbuf_tensor("ssq", [128, 8], F32))
        rstdq = An(nc.sbuf_tensor("rstdq", [128, 8], F32))
        rstdk = An(nc.sbuf_tensor("rstdk", [128, 8], F32))
        qhat = An(nc.sbuf_tensor("qhat", [128, 1024], BF16))
        khat = An(nc.sbuf_tensor("khat", [128, 1024], BF16))
        vhat = An(nc.sbuf_tensor("vhat", [128, 1024], BF16))
        qT_r = An(nc.sbuf_tensor("qT_r", [128, 1024], BF16))
        kT_r = An(nc.sbuf_tensor("kT_r", [128, 1024], BF16))
        aT_r = An(nc.sbuf_tensor("aT_r", [128, 1024], BF16))
        yT_r = An(nc.sbuf_tensor("yT_r", [128, 1024], BF16))
        sc_m = An(nc.sbuf_tensor("sc_m", [128, 1024], F32))
        attb = An(nc.sbuf_tensor("attb", [128, 1024], BF16))
        state = An(nc.sbuf_tensor("state", [128, 2048], F32))
        st_bf = An(nc.sbuf_tensor("st_bf", [128, 2048], BF16))
        r_ring = An(nc.sbuf_tensor("r_ring", [128, 2 * 1024], F32))
        lngb = An(nc.sbuf_tensor("lngb", [8, 128], F32))
        zabs = An(nc.sbuf_tensor("zabs", [8, 128], F32))
        lseg = An(nc.sbuf_tensor("lseg", [8, 2], F32))
        gseg = An(nc.sbuf_tensor("gseg", [8, 2], F32))
        g128 = An(nc.sbuf_tensor("g128", [128, 16], F32))
        mx8 = An(nc.sbuf_tensor("mx8", [128, 8], F32))
        sm8 = An(nc.sbuf_tensor("sm8", [128, 8], F32))
        eq8 = An(nc.sbuf_tensor("eq8", [128, 8], F32))

        pq = En(nc.psum_tensor("pq", [128, 1024], F32))
        ptr = En(nc.psum_tensor("ptr", [128, 1024], BF16))
        psc = En(nc.psum_tensor("psc", [128, 1024], F32))
        py = En(nc.psum_tensor("py", [128, 1024], F32))
        pmsc = En(nc.psum_tensor("pmsc", [128, 512], F32))

        LD_DONE = ["ld_done"]

        def attn_phase():
            for e in (T, V, S):
                e.wait("ld_done", rec)
                e.wait("ldg_done", rec)
            S.op("activation", small[0:1, 15:16], qk_sb[0:1, 0:1], AF.Ln, mark="lnqk_s", rec=rec)
            T.wait("lnqk_s", rec)
            T.mm(pmsc[:, 320:321], onesf_sb[0:1, :], small[0:1, 15:16], start=True, stop=True,
                 mark="lnqk_t", rec=rec)
            S.wait("lnqk_t", rec)
            S.op("copy", lnqk_sb[:, 0:1], pmsc[:, 320:321], mark="lnqk128", rec=rec)

            def sy_emit(k):
                if k >= 1:
                    jp = order[k - 1]
                    SY.wait(("gex", jp), rec)
                    SY.dma(gam_d[jp:jp + 1, :].rearrange("o (h c) -> (o h) c", h=8),
                           gseg[:, :], "gw", rec)
                    if k >= 2:
                        SY.wait(("so", order[k - 2]), rec)
                    SY.wait_dma("gw", 16 * k, rec)
                    src = bass.AP(gam_d.tensor, jp * 16, [[0, 128], [2, 8], [1, 2]])
                    g128v = g128[:, :].rearrange("p (h c) -> p h c", h=8)
                    rec.dn["gr"] = rec.dn.get("gr", 0) + 16
                    if SY.eng is not None:
                        SY.eng.dma_start(g128v, src).then_inc(rec.dsems["gr"], 16)
                    rec.marks[("g128", jp)] = ("DMA:gr", rec.dn["gr"])
                if k == 0:
                    for kj in range(min(2, NTOT)):
                        jj = order[kj]
                        SY.dma(x_sb[:, (kj % 3) * 1024:(kj % 3 + 1) * 1024],
                               inp["x"][jj // ntt, (jj % ntt) * 128:(jj % ntt + 1) * 128, :],
                               "xs%d" % (kj % 3), rec)
                kj = k + 2
                if kj < NTOT:
                    jj = order[kj]
                    if k >= 1:
                        SY.wait(("Radd", order[k - 1]), rec)
                    SY.dma(x_sb[:, (kj % 3) * 1024:(kj % 3 + 1) * 1024],
                           inp["x"][jj // ntt, (jj % ntt) * 128:(jj % ntt + 1) * 128, :],
                           "xs%d" % (kj % 3), rec)
                if k >= 1:
                    jp = order[k - 1]
                    SY.wait(("Radd", jp), rec)
                    SY.dma(r_d[jp * 128:(jp + 1) * 128, :],
                           r_ring[:, ((k - 1) % 2) * 1024:((k - 1) % 2 + 1) * 1024],
                           "rst%d" % ((k - 1) % 2), rec)
                    c0 = (k - 1) * len(pc_chunks) // NTOT
                    c1 = k * len(pc_chunks) // NTOT
                    if c1 > c0:
                        G.wait(("Radd", jp), rec)
                        for dst, src, key in pc_chunks[c0:c1]:
                            G.dma(dst, src, key, rec)

            def ln_emit(k, i, ip):
                sl = k % 3
                xs = x_sb[:, sl * 1024:(sl + 1) * 1024]
                V.wait_dma("xs%d" % sl, 16 * (k // 3 + 1), rec)
                V.op("reduce_sum", small[:, 0:1], xs, axis=AX.X, mark=("s1", i), rec=rec)
                S.wait_dma("xs%d" % sl, 16 * (k // 3 + 1), rec)
                if ip is not None:
                    S.wait(("trxn", ip), rec)
                S.op("activation", junk[:, :], xs, AF.Square, accum_out=small[:, 1:2],
                     mark=("s2", i), rec=rec)
                V.op("tensor_scalar_mul", small[:, 2:3], small[:, 0:1], 1.0 / D)
                V.wait(("s2", i), rec)
                V.op("tensor_scalar_mul", small[:, 3:4], small[:, 1:2], 1.0 / D)
                V.op("tensor_mul", small[:, 4:5], small[:, 2:3], small[:, 2:3])
                V.op("tensor_sub", small[:, 5:6], small[:, 3:4], small[:, 4:5], mark=("var", i), rec=rec)
                S.wait(("var", i), rec)
                S.op("activation", small[:, 6:7], small[:, 5:6], AF.Ln, bias=epsc_sb[:, 0:1])
                S.op("activation", small[:, 7:8], small[:, 6:7], AF.Exp, scale=-0.5,
                     mark=("rstd", i), rec=rec)
                V.wait(("rstd", i), rec)
                V.op("tensor_scalar", junk[:, :], xs, small[:, 2:3], small[:, 7:8],
                     ALU.subtract, ALU.mult, mark=("xn", i), rec=rec)
                T.wait(("xn", i), rec)
                if ip is not None:
                    T.wait(("kTe", ip), rec)
                for dc in range(8):
                    T.op("transpose", ptr[:, dc * 128:(dc + 1) * 128],
                         junk[:, dc * 128:(dc + 1) * 128], ident_sb[:, :],
                         mark=("trxn", i) if dc == 7 else None, rec=rec)
                S.wait(("trxn", i), rec)
                S.op("copy",
                     xnT[:, :].rearrange("p (dc t) -> p dc t", dc=8)[:, :, i * 128:i * 128 + 128],
                     ptr[:, :].rearrange("p (dc t) -> p dc t", dc=8),
                     mark=("xnev", i), rec=rec)

            def smax_emit(i, ip, inx):
                V.wait(("sc", i), rec)
                if ip is not None:
                    V.wait(("exp", ip), rec)
                V.op("scalar_tensor_tensor", sc_m[:, :].rearrange("p (s x) -> p s x", s=8),
                     psc[:, :].rearrange("p (s x) -> p s x", s=8), RT2,
                     cmask_sb[:, :].unsqueeze(1).broadcast_to((128, 8, 128)),
                     ALU.mult, ALU.add, mark=("msk", i), rec=rec)
                V.op("reduce_max", mx8[:, :], sc_m[:, :].rearrange("p (s x) -> p s x", s=8),
                     axis=AX.X)
                V.op("scalar_tensor_tensor", sc_m[:, :].rearrange("p (s x) -> p s x", s=8),
                     sc_m[:, :].rearrange("p (s x) -> p s x", s=8), 1.0,
                     mx8[:, :].unsqueeze(2).broadcast_to((128, 8, 128)),
                     ALU.mult, ALU.subtract, mark=("sub", i), rec=rec)
                S.wait(("sub", i), rec)
                if ip is not None:
                    S.wait(("att", ip), rec)
                S.op("activation", sc_m[:, :], sc_m[:, :], AF.Exp, mark=("exp", i), rec=rec)
                V.wait(("exp", i), rec)
                V.op("reduce_sum", sm8[:, :], sc_m[:, :].rearrange("p (s x) -> p s x", s=8),
                     axis=AX.X)
                V.op("reciprocal", sm8[:, :], sm8[:, :])
                if ip is not None:
                    V.wait(("tra", ip), rec)
                V.op("tensor_tensor", attb[:, :].rearrange("p (s x) -> p s x", s=8),
                     sc_m[:, :].rearrange("p (s x) -> p s x", s=8),
                     sm8[:, :].unsqueeze(2).broadcast_to((128, 8, 128)),
                     ALU.mult, mark=("att", i), rec=rec)
                S.wait(("zg", i), rec)
                S.op("activation", zabs[:, :], pmsc[0:8, 0:128], AF.Abs)
                S.op("activation", zabs[:, :], zabs[:, :], AF.Exp, scale=-1.0)
                S.op("activation", zabs[:, :], zabs[:, :], AF.Ln, bias=epsc_sb[0:8, 2:3],
                     mark=("lnl", i), rec=rec)
                V.wait(("lnl", i), rec)
                V.op("tensor_scalar_min", lngb[:, :], pmsc[0:8, 0:128], 0.0, mark=("zmin", i), rec=rec)
                V.op("tensor_sub", lngb[:, :], lngb[:, :], zabs[:, :])
                V.op("reduce_sum", lseg[:, :], lngb[:, :].rearrange("h (c x) -> h c x", c=2),
                     axis=AX.X, mark=("lns", i), rec=rec)
                S.wait(("lns", i), rec)
                S.wait_dma("gw", max(0, rec.dn.get("gw", 0) - 16), rec)
                S.op("activation", gseg[:, :], lseg[:, :], AF.Exp, mark=("gex", i), rec=rec)
                T.wait(("att", i), rec)
                if inx is not None:
                    T.wait(("xnev", inx), rec)
                for h in range(8):
                    T.op("transpose", ptr[:, h * 128:(h + 1) * 128],
                         attb[:, h * 128:(h + 1) * 128], ident_sb[:, :],
                         mark=("tra", i) if h == 7 else None, rec=rec)
                S.wait(("tra", i), rec)
                if ip is not None:
                    S.wait(("yin", ip), rec)
                S.op("copy", aT_r[:, :], ptr[:, :], mark=("aTe", i), rec=rec)
                V.wait(("gf", i), rec)
                V.op("reduce_max", mx8[:, 0:1], pmsc[:, 128:136], axis=AX.X)
                V.op("tensor_scalar", eq8[:, :], pmsc[:, 128:136], mx8[:, 0:1], None,
                     ALU.is_equal, mark=("ohq", i), rec=rec)
                V.op("tensor_mul", eq8[:, :], eq8[:, :], pow2_sb[:, :])
                V.op("reduce_max", mx8[:, 1:2], eq8[:, :], axis=AX.X)
                V.op("tensor_scalar", oh[:, i * 8:(i + 1) * 8], eq8[:, :], mx8[:, 1:2], None,
                     ALU.is_equal, mark=("ohe", i), rec=rec)
                S.wait(("ohe", i), rec)
                if ip is not None:
                    S.wait(("ohT", ip), rec)
                S.op("copy", ohb[:, :], oh[:, i * 8:(i + 1) * 8], mark=("ohb", i), rec=rec)
                T.wait(("ohb", i), rec)
                if ip is not None:
                    T.wait(("ohTe", ip), rec)
                T.op("transpose", pmsc[0:8, 384:448].bitcast(BF16), ohb[:, :],
                     ident_sb[:, :], mark=("ohT", i), rec=rec)
                S.wait(("ohT", i), rec)
                if ip is not None:
                    S.wait(("wo", ip), rec)
                S.op("copy", ohT_sb[:, :], pmsc[0:8, 384:448].bitcast(BF16), mark=("ohTe", i), rec=rec)

            def scan_emit(i, ip, rp, sl):
                b, tt = divmod(i, ntt)
                stv = state[:, b * 1024:(b + 1) * 1024]
                stb = st_bf[:, b * 1024:(b + 1) * 1024]
                xs = x_sb[:, sl * 1024:(sl + 1) * 1024]
                if tt == 0:
                    V.op("memset", stv, 0.0)
                    V.op("memset", stb, 0.0, mark=("st0", b), rec=rec)
                T.wait(("aTe", i), rec)
                T.wait(("vev", i), rec)
                if ip is not None:
                    T.wait(("yTe", ip), rec)
                if tt == 0:
                    T.wait(("st0", b), rec)
                else:
                    T.wait(("sbo", i - 1), rec)
                for h in range(8):
                    T.mm(py[:, h * 128:(h + 1) * 128], vhat[:, h * 128:(h + 1) * 128],
                         aT_r[:, h * 128:(h + 1) * 128], start=True, stop=True,
                         skip_group_check=True,
                         mark=("yin", i) if h == 7 else None, rec=rec)
                for h in range(8):
                    T.mm(py[:, h * 128:h * 128 + 64], stb[:, h * 128:(h + 1) * 128],
                         qT_r[:, h * 128:h * 128 + 64], start=False, stop=True,
                         skip_group_check=True, rec=rec)
                T.wait(("msk", i), rec)
                for h in range(8):
                    T.mm(psc[:, h * 128:(h + 1) * 128], khat[0:64, h * 128:(h + 1) * 128],
                         vhat[0:64, h * 128:(h + 1) * 128], start=True, stop=True,
                         mark=("kve", i) if h == 7 else None, rec=rec)
                V.wait(("kve", i), rec)
                V.wait(("g128", i), rec)
                if tt == 0:
                    V.wait(("st0", b), rec)
                for hh in range(2):
                    hs = slice(hh * 512, (hh + 1) * 512)
                    V.op("tensor_tensor", stv[:, hs].rearrange("p (s x) -> p s x", s=4),
                         stv[:, hs].rearrange("p (s x) -> p s x", s=4),
                         g128[:, :].rearrange("p (h c) -> p h c", h=8)[:, hh * 4:hh * 4 + 4, 0:1]
                         .broadcast_to((128, 4, 128)),
                         ALU.mult)
                    V.op("tensor_add", stv[:, hs], stv[:, hs], psc[:, hs],
                         mark=("seA", i) if hh == 0 else ("se", i), rec=rec)
                    V.op("tensor_copy", stb[:, hs], stv[:, hs],
                         mark=("sbeA", i) if hh == 0 else ("sbe", i), rec=rec)
                T.wait(("sbeA", i), rec)
                for h in range(4):
                    T.mm(py[:, h * 128 + 64:h * 128 + 128], stb[:, h * 128:(h + 1) * 128],
                         qT_r[:, h * 128 + 64:h * 128 + 128], start=False, stop=True,
                         skip_group_check=True, rec=rec)
                T.wait(("sbe", i), rec)
                for h in range(4, 8):
                    T.mm(py[:, h * 128 + 64:h * 128 + 128], stb[:, h * 128:(h + 1) * 128],
                         qT_r[:, h * 128 + 64:h * 128 + 128], start=False, stop=True,
                         skip_group_check=True,
                         mark=("yco", i) if h == 7 else None, rec=rec)
                T.wait(("se", i), rec)
                for h in range(8):
                    T.mm(psc[:, h * 128:(h + 1) * 128], khat[64:128, h * 128:(h + 1) * 128],
                         vhat[64:128, h * 128:(h + 1) * 128], start=True, stop=True,
                         mark=("kvo", i) if h == 7 else None, rec=rec)
                V.wait(("kvo", i), rec)
                V.op("tensor_tensor", stv.rearrange("p (s x) -> p s x", s=8),
                     stv.rearrange("p (s x) -> p s x", s=8),
                     g128[:, :].rearrange("p (h c) -> p h c", h=8)[:, :, 1:2].broadcast_to((128, 8, 128)),
                     ALU.mult)
                V.op("tensor_add", stv, stv, psc[:, :], mark=("so", i), rec=rec)
                V.op("tensor_copy", stb, stv, mark=("sbo", i), rec=rec)
                S.wait(("yco", i), rec)
                if ip is not None:
                    S.wait(("wo", ip), rec)
                S.op("copy", yT_r[:, :], py[:, :], mark=("yTe", i), rec=rec)
                T.wait(("yTe", i), rec)
                T.wait(("qh", i), rec)
                T.wait(("ohTe", i), rec)
                for half in range(2):
                    for ec in range(8):
                        T.mm(pq[:, half * 512:(half + 1) * 512], yT_r[:, ec * 128:(ec + 1) * 128],
                             WT["wot"][:, ec * 1024 + half * 512:ec * 1024 + (half + 1) * 512],
                             start=(ec == 0), stop=False, rec=rec)
                    T.mm(pq[:, half * 512:(half + 1) * 512], ohT_sb[:, :],
                         b2all[:, half * 512:(half + 1) * 512],
                         start=False, stop=True,
                         mark=("wo", i) if half == 1 else ("woA", i), rec=rec)
                prev_rst = rec.dn.get("rst%d" % rp, 0) - 16
                for half in range(2):
                    V.wait(("wo", i) if half == 1 else ("woA", i), rec)
                    if half == 0 and prev_rst > 0:
                        V.wait_dma("rst%d" % rp, prev_rst, rec)
                    hs = slice(half * 512, (half + 1) * 512)
                    V.op("tensor_add", r_ring[:, rp * 1024 + half * 512:rp * 1024 + (half + 1) * 512],
                         pq[:, hs], xs[:, hs],
                         mark=("Radd", i) if half == 1 else ("RaddA", i), rec=rec)

            def proj_emit(i, ip):
                t0 = i * 128
                T.wait(("xnev", i), rec)
                for half in range(2):
                    if ip is not None:
                        T.wait(("Radd", ip) if half == 1 else ("RaddA", ip), rec)
                    for dc in range(8):
                        T.mm(pq[:, half * 512:(half + 1) * 512],
                             xnT[:, dc * NT + t0:dc * NT + t0 + 128],
                             WT["wqt"][:, dc * 1024 + half * 512:dc * 1024 + (half + 1) * 512],
                             start=(dc == 0), stop=(dc == 7),
                             mark=("qp", i) if (half == 1 and dc == 7) else None, rec=rec)
                if ip is not None:
                    T.wait(("zmin", ip), rec)
                T.mm(pmsc[0:8, 0:128], bga_sb[0:1, :], onesf_sb[0:1, :], start=True, stop=False)
                for dc in range(8):
                    T.mm(pmsc[0:8, 0:128], wgat_sb[:, dc * 8:(dc + 1) * 8],
                         xnT[:, dc * NT + t0:dc * NT + t0 + 128],
                         start=False, stop=(dc == 7), mark=("zg", i) if dc == 7 else None, rec=rec)
                if ip is not None:
                    T.wait(("ohq", ip), rec)
                T.mm(pmsc[:, 128:136], onesf_sb[0:1, :], bgf_sb[0:1, :], start=True, stop=False)
                for dc in range(8):
                    T.mm(pmsc[:, 128:136], xnT[:, dc * NT + t0:dc * NT + t0 + 128],
                         wgf_sb[:, dc * 8:(dc + 1) * 8],
                         start=False, stop=(dc == 7), mark=("gf", i) if dc == 7 else None, rec=rec)
                S.wait(("qp", i), rec)
                if ip is not None:
                    S.wait(("kss", ip), rec)
                S.op("activation", sqq[:, 0:512], pq[:, 0:512], AF.Square)
                S.op("activation", sqq[:, 512:1024], pq[:, 512:1024], AF.Square,
                     mark=("qsq", i), rec=rec)
                V.wait(("qsq", i), rec)
                V.op("reduce_sum", ssq[:, :], sqq[:, :].rearrange("p (s x) -> p s x", s=8),
                     axis=AX.X, mark=("qss", i), rec=rec)
                S.wait(("qss", i), rec)
                S.op("activation", rstdq[:, :], ssq[:, :], AF.Ln, scale=1.0 / 128, bias=epsc_sb[:, 1:2])
                S.wait("lnqk128", rec)
                S.op("activation", rstdq[:, :], rstdq[:, :], AF.Exp, scale=-0.5,
                     bias=lnqk_sb[:, 0:1], mark=("rq", i), rec=rec)
                V.wait(("rq", i), rec)
                if ip is not None:
                    V.wait(("trq", ip), rec)
                V.op("tensor_tensor", qhat[:, :].rearrange("p (s x) -> p s x", s=8),
                     pq[:, :].rearrange("p (s x) -> p s x", s=8),
                     rstdq[:, :].unsqueeze(2).broadcast_to((128, 8, 128)),
                     ALU.mult, mark=("qh", i), rec=rec)
                if ip is not None:
                    T.wait(("so", ip), rec)
                for half in range(2):
                    for dc in range(8):
                        T.mm(psc[:, half * 512:(half + 1) * 512],
                             xnT[:, dc * NT + t0:dc * NT + t0 + 128],
                             WT["wkt"][:, dc * 1024 + half * 512:dc * 1024 + (half + 1) * 512],
                             start=(dc == 0), stop=(dc == 7),
                             mark=("kp", i) if (half == 1 and dc == 7) else None, rec=rec)
                S.wait(("kp", i), rec)
                S.wait(("qss", i), rec)
                S.op("activation", sqq[:, 0:512], psc[:, 0:512], AF.Square)
                S.op("activation", sqq[:, 512:1024], psc[:, 512:1024], AF.Square,
                     mark=("ksq", i), rec=rec)
                V.wait(("ksq", i), rec)
                V.op("reduce_sum", ssq[:, :], sqq[:, :].rearrange("p (s x) -> p s x", s=8),
                     axis=AX.X, mark=("kss", i), rec=rec)
                S.wait(("kss", i), rec)
                S.op("activation", rstdk[:, :], ssq[:, :], AF.Ln, scale=1.0 / 128, bias=epsc_sb[:, 1:2])
                S.op("activation", rstdk[:, :], rstdk[:, :], AF.Exp, scale=-0.5,
                     mark=("rk", i), rec=rec)
                V.wait(("rk", i), rec)
                if ip is not None:
                    V.wait(("trk", ip), rec)
                    V.wait(("kvo", ip), rec)
                V.op("tensor_tensor", khat[:, :].rearrange("p (s x) -> p s x", s=8),
                     psc[:, :].rearrange("p (s x) -> p s x", s=8),
                     rstdk[:, :].unsqueeze(2).broadcast_to((128, 8, 128)),
                     ALU.mult, mark=("kh", i), rec=rec)
                if ip is not None:
                    T.wait(("yTe", ip), rec)
                for half in range(2):
                    for dc in range(8):
                        T.mm(py[:, half * 512:(half + 1) * 512],
                             xnT[:, dc * NT + t0:dc * NT + t0 + 128],
                             WT["wvt"][:, dc * 1024 + half * 512:dc * 1024 + (half + 1) * 512],
                             start=(dc == 0), stop=(dc == 7),
                             mark=("vp", i) if (half == 1 and dc == 7) else None, rec=rec)
                S.wait(("vp", i), rec)
                if ip is not None:
                    S.wait(("kvo", ip), rec)
                S.op("copy", vhat[:, :], py[:, :], mark=("vev", i), rec=rec)
                T.wait(("qh", i), rec)
                if ip is not None:
                    T.wait(("aTe", ip), rec)
                for dc in range(8):
                    T.op("transpose", ptr[:, dc * 128:(dc + 1) * 128],
                         qhat[:, dc * 128:(dc + 1) * 128], ident_sb[:, :],
                         mark=("trq", i) if dc == 7 else None, rec=rec)
                S.wait(("trq", i), rec)
                if ip is not None:
                    S.wait(("yco", ip), rec)
                S.op("copy", qT_r[:, :], ptr[:, :], mark=("qTe", i), rec=rec)
                T.wait(("qTe", i), rec)
                T.wait(("kh", i), rec)
                for dc in range(8):
                    T.op("transpose", ptr[:, dc * 128:(dc + 1) * 128],
                         khat[:, dc * 128:(dc + 1) * 128], ident_sb[:, :],
                         mark=("trk", i) if dc == 7 else None, rec=rec)
                S.wait(("trk", i), rec)
                if ip is not None:
                    S.wait(("sc", ip), rec)
                S.op("copy", kT_r[:, :], ptr[:, :], mark=("kTe", i), rec=rec)
                T.wait(("kTe", i), rec)
                T.wait(("kh", i), rec)
                for h in range(8):
                    T.mm(psc[:, h * 128:(h + 1) * 128], qT_r[:, h * 128:(h + 1) * 128],
                         kT_r[:, h * 128:(h + 1) * 128], start=True, stop=True,
                         mark=("sc", i) if h == 7 else None, rec=rec)

            order = [(k % nb) * ntt + (k // nb) for k in range(NTOT)]
            for k in range(NTOT):
                i = order[k]
                ip = order[k - 1] if k >= 1 else None
                ipp = order[k - 2] if k >= 2 else None
                sy_emit(k)
                ln_emit(k, i, ip)
                if k >= 1:
                    smax_emit(ip, ipp, i)
                    scan_emit(ip, ipp, (k - 1) % 2, (k - 1) % 3)
                proj_emit(i, ip)
            k = NTOT
            iL = order[NTOT - 1]
            sy_emit(k)
            smax_emit(iL, order[NTOT - 2] if NTOT >= 2 else None, None)
            scan_emit(iL, order[NTOT - 2] if NTOT >= 2 else None, (NTOT - 1) % 2, (NTOT - 1) % 3)
            SY.wait_dma("rst0", rec.dn["rst0"], rec)
            SY.wait_dma("rst1", rec.dn.get("rst1", 0), rec)

        attn_ctx.close()
        w1r = En(nc.sbuf_tensor("w1r", [128, 8 * 1024], BF16))
        w2r = En(nc.sbuf_tensor("w2r", [128, 8 * 1024], BF16))
        hT = En(nc.sbuf_tensor("hT", [128, NJC * 512], BF16))
        outb = [En(nc.sbuf_tensor("outb%d" % p, [128, 4 * 1024], F32)) for p in range(2)]
        psf = [pq[:, 0:512], pq[:, 512:1024], psc[:, 0:512], psc[:, 512:1024],
               py[:, 0:512], py[:, 512:1024], pmsc[:, 0:512], ptr[:, :].bitcast(F32)]

        def ffn_phase():
            U = NTB * NE
            NL = NTOT - 1
            for m in [("wo", NL), ("ohe", NL), ("yTe", NL), ("sbo", NL), ("g128", NL), ("Radd", NL)]:
                SY.wait(m, rec)
            for u in range(U):
                tb, e = divmod(u, NE)
                for jc in range(NJC):
                    g = u * NJC + jc
                    if tb == 0 and jc == 0:
                        SY.wait_dma("pc%d" % e, 128, rec)
                    if g >= 8:
                        SY.wait(("f1", g - 8), rec)
                    SY.dma(w1r[:, (g % 8) * 1024:(g % 8 + 1) * 1024], w1bf_d[e, jc],
                           "w1s%d" % (g % 8), rec)
            for m in [("wo", NL), ("ohe", NL), ("yTe", NL), ("sbo", NL), ("g128", NL), ("Radd", NL)]:
                G.wait(m, rec)
            G.wait_dma("rst0", rec.dn["rst0"], rec)
            G.wait_dma("rst1", rec.dn["rst1"], rec)
            for u in range(U):
                tb, e = divmod(u, NE)
                for jc in range(NJC):
                    g = u * NJC + jc
                    if tb == 0 and jc == 0:
                        G.wait_dma("pc%d" % e, 128, rec)
                    if g >= 8:
                        G.wait(("f2", g - 8), rec)
                    G.dma(w2r[:, (g % 8) * 1024:(g % 8 + 1) * 1024],
                          w2bf_d[e, jc * 128:(jc + 1) * 128, :], "w2s%d" % (g % 8), rec)
            for u in range(U):
                tb, e = divmod(u, NE)
                if e == 0:
                    T.wait(("xnev", min((tb + 1) * 4, NTOT) - 1), rec)
                for jc in range(NJC):
                    g = u * NJC + jc
                    T.wait_dma("w1s%d" % (g % 8), 16 * (g // 8 + 1), rec)
                    bank = jc % 2
                    if jc >= 2:
                        T.wait(("fs", g - 2), rec)
                    elif u > 0:
                        T.wait(("fv", (u - 1) * 8 + bank), rec)
                    for dc in range(8):
                        T.mm(psf[bank],
                             w1r[:, (g % 8) * 1024 + dc * 128:(g % 8) * 1024 + (dc + 1) * 128],
                             xnT[:, dc * NT + tb * 512:dc * NT + (tb + 1) * 512],
                             start=(dc == 0), stop=(dc == 7),
                             mark=("f1", g) if dc == 7 else None, rec=rec)
                for jc in range(NJC):
                    g = u * NJC + jc
                    T.wait_dma("w2s%d" % (g % 8), 16 * (g // 8 + 1), rec)
                    T.wait(("fs", g), rec)
                    if jc == 0:
                        if u > 0:
                            T.wait(("fv", u * 8 - 1), rec)
                        ots = [2, 3, 4, 5, 6, 7, 0, 1]
                    else:
                        ots = list(range(8))
                    for ot in ots:
                        if jc == 0 and ot == 0:
                            T.wait(("fs", u * NJC + NJC - 1), rec)
                        tsub, dsl = divmod(ot, 2)
                        if ot == ots[-1]:
                            mk = ("f2", g)
                        elif jc == NJC - 1:
                            mk = ("f2o", u * 8 + ot)
                        else:
                            mk = None
                        T.mm(psf[ot],
                             hT[:, jc * 512 + tsub * 128:jc * 512 + (tsub + 1) * 128],
                             w2r[:, (g % 8) * 1024 + dsl * 512:(g % 8) * 1024 + (dsl + 1) * 512],
                             start=(jc == 0), stop=(jc == NJC - 1),
                             mark=mk, rec=rec)
            S.wait(("ohe", NL), rec)
            S.wait(("wo", NL), rec)
            S.wait(("sbo", NL), rec)
            S.wait(("Radd", NL), rec)
            for u in range(U):
                tb, e = divmod(u, NE)
                if e == 0:
                    if tb >= 2:
                        S.wait_dma("fst%d" % (tb % 2), 16 * ((tb - 2) // 2 + 1), rec)
                    need = [0, 0]
                    for g in range(tb * 4, tb * 4 + 4):
                        kg = (g % ntt) * nb + (g // ntt)
                        need[kg % 2] = max(need[kg % 2], kg // 2 + 1)
                    for par in range(2):
                        if need[par]:
                            S.wait_dma("rst%d" % par, 16 * need[par], rec)
                    S.dma(outb[tb % 2][:, :].rearrange("p (ts d) -> p ts d", ts=4),
                          r_d.rearrange("(a ts p) d -> a p ts d", p=128, ts=4)[tb],
                          "fr%d" % (tb % 2), rec)
                if e == min(1, NE - 1) and tb > 0:
                    S.wait(("fvU", tb - 1), rec)
                    S.dma(out_flat.rearrange("(a ts p) d -> a p ts d", p=128, ts=4)[tb - 1],
                          outb[(tb - 1) % 2][:, :].rearrange("p (ts d) -> p ts d", ts=4),
                          "fst%d" % ((tb - 1) % 2), rec)
                for jc in range(NJC):
                    g = u * NJC + jc
                    S.wait(("f1", g), rec)
                    if u > 0:
                        S.wait(("f2", (u - 1) * NJC + jc), rec)
                    S.op("activation", hT[:, jc * 512:(jc + 1) * 512], psf[jc % 2],
                         AF.Relu, bias=b1sb[:, e * NJC + jc:e * NJC + jc + 1],
                         mark=("fs", g), rec=rec)
            S.wait(("fvU", NTB - 1), rec)
            S.dma(out_flat.rearrange("(a ts p) d -> a p ts d", p=128, ts=4)[NTB - 1],
                  outb[(NTB - 1) % 2][:, :].rearrange("p (ts d) -> p ts d", ts=4),
                  "fst%d" % ((NTB - 1) % 2), rec)
            for par in range(2):
                tot = len([t_ for t_ in range(NTB) if t_ % 2 == par])
                if tot:
                    S.wait_dma("fst%d" % par, 16 * tot, rec)
            for u in range(U):
                tb, e = divmod(u, NE)
                ob = outb[tb % 2]
                if e == 0:
                    V.wait_dma("fr%d" % (tb % 2), 16 * (tb // 2 + 1), rec)
                for ot in range(8):
                    V.wait(("f2o", u * 8 + ot) if ot < 7 else ("f2", (u + 1) * NJC - 1), rec)
                    tsub, dsl = divmod(ot, 2)
                    obs = ob[:, tsub * 1024 + dsl * 512:tsub * 1024 + (dsl + 1) * 512]
                    V.op("scalar_tensor_tensor", obs, psf[ot],
                         oh[:, (tb * 4 + tsub) * 8 + e:(tb * 4 + tsub) * 8 + e + 1],
                         obs, ALU.mult, ALU.add,
                         mark=("fv", u * 8 + ot), rec=rec)
                if e == NE - 1:
                    rec.marks[("fvU", tb)] = ("V", V.n)

        out_flat = out_d.rearrange("b t d -> (b t) d")

        def run_program():
            phase0()
            attn_phase()
            ffn_phase()

        reset(False)
        run_program()

        reset(True)
        live = {e.name: (e.eng, e.sem) for e in (T, V, S, SY, G)}

        def emit_for(target):
            for e in (T, V, S, SY, G):
                if e.name == target:
                    e.eng, e.sem = live[e.name]
                else:
                    e.eng, e.sem = None, None
                e.n = 0
            rec.dn = {}
            run_program()

        with nc.Block("mk", no_gpsimd_drain=True) as blk:
            blk.sync(lambda _e: emit_for("SY"))
            blk.gpsimd(lambda _e: emit_for("G"))
            blk.tensor(lambda _e: emit_for("T"))
            blk.scalar(lambda _e: emit_for("S"))
            blk.vector(lambda _e: emit_for("V"))

    return nc


_PROG_CACHE = {}


def kernel(**inputs):
    from concourse.bass_utils import run_bass_kernel_spmd

    nb, ntt = 2, 16
    if "prog" not in _PROG_CACHE:
        _PROG_CACHE["prog"] = build(nb, ntt)
    nc = _PROG_CACHE["prog"]
    in_maps = [host_prep(inputs, nb, ntt, core * nb) for core in range(8)]
    res = run_bass_kernel_spmd(nc, in_maps, core_ids=list(range(8)))
    return np.concatenate([np.asarray(r["out"]) for r in res.results], axis=0)


# version 09
# speedup vs baseline: 1.1727x; 1.1727x over previous
import numpy as np
import ml_dtypes
import concourse.bass as bass
import concourse.mybir as mybir
from contextlib import ExitStack

F32 = mybir.dt.float32
BF16 = mybir.dt.bfloat16
AF = mybir.ActivationFunctionType
ALU = mybir.AluOpType
AX = mybir.AxisListType

D = 1024
HF = 4096
NE = 8
NJC = 32
RT2 = float(1.0 / np.sqrt(128.0))


class Eng:
    def __init__(self, name, auto_self=False):
        self.name = name
        self.eng = None
        self.sem = None
        self.n = 0
        self.auto_self = auto_self

    def op(self, fn, *args, mark=None, rec=None, **kw):
        if self.auto_self and self.eng is not None and self.n > 0:
            self.eng.wait_ge(self.sem, self.n)
        self.n += 1
        if self.eng is not None:
            getattr(self.eng, fn)(*args, **kw).then_inc(self.sem, 1)
        if mark is not None:
            rec.marks[mark] = (self.name, self.n)

    def mm(self, *args, mark=None, rec=None, **kw):
        self.op("matmul", *args, mark=mark, rec=rec, **kw)

    def dma(self, dst, src, key, rec, mark=None):
        rec.dn[key] = rec.dn.get(key, 0) + 16
        if self.eng is not None:
            self.eng.dma_start(dst, src).then_inc(rec.dsems[key], 16)
        if mark is not None:
            rec.marks[mark] = ("DMA:" + key, rec.dn[key])

    def wait(self, mark, rec):
        if self.eng is None:
            return
        src, val = rec.marks[mark]
        if src.startswith("DMA:"):
            self.eng.wait_ge(rec.dsems[src[4:]], val)
        else:
            self.eng.wait_ge(rec.engsems[src], val)

    def wait_dma(self, key, val, rec):
        if self.eng is not None:
            self.eng.wait_ge(rec.dsems[key], val)


class Rec:
    def __init__(self):
        self.marks = {}
        self.dsems = {}
        self.dn = {}
        self.engs = {}
        self.engsems = {}


def host_prep(inputs, nb, ntt, core_b0):
    f32 = np.float32
    TT = ntt * 128
    d = {}
    d["x"] = np.ascontiguousarray(np.asarray(inputs["L0_attn_x"])[core_b0:core_b0 + nb, :TT]).astype(f32, copy=False)
    d["wqt"] = np.ascontiguousarray(np.asarray(inputs["L0_attn_Wq"]).T)
    d["wkt"] = np.ascontiguousarray(np.asarray(inputs["L0_attn_Wk"]).T)
    d["wvt"] = np.ascontiguousarray(np.asarray(inputs["L0_attn_Wv"]).T)
    d["wot"] = np.ascontiguousarray(np.asarray(inputs["L0_attn_Wo"]).T)
    d["wgat"] = np.ascontiguousarray(np.asarray(inputs["L0_attn_Wg"]).T)
    d["bga"] = np.ascontiguousarray(np.asarray(inputs["L0_attn_bg"], f32).reshape(1, 8))
    d["qk"] = np.ascontiguousarray(np.asarray(inputs["L0_attn_qk_gain"], f32).reshape(1, 1))
    d["wgf"] = np.ascontiguousarray(np.asarray(inputs["L0_ffn_Wg"]))
    d["bgf"] = np.ascontiguousarray(np.asarray(inputs["L0_ffn_bg"], f32).reshape(1, 8))
    d["w1p"] = np.ascontiguousarray(
        np.asarray(inputs["L0_ffn_W1"]).reshape(NE, 8, 128, NJC, 128).transpose(0, 3, 2, 1, 4).reshape(NE, NJC, 128, 1024))
    d["w2n"] = np.ascontiguousarray(np.asarray(inputs["L0_ffn_W2"]))
    d["b1c"] = np.ascontiguousarray(np.asarray(inputs["L0_ffn_b1"]).reshape(NE, NJC, 128).transpose(0, 2, 1))
    d["b2r"] = np.ascontiguousarray(np.asarray(inputs["L0_ffn_b2"]))
    t_i = np.arange(128)[:, None]
    s_i = np.arange(128)[None, :]
    blockdiag = (t_i // 64 == s_i // 64) & (s_i <= t_i)
    d["cmask"] = np.where(blockdiag, 0.0, -1e30).astype(f32)
    d["pow2"] = np.broadcast_to(2.0 ** -np.arange(8, dtype=f32), (128, 8)).copy()
    d["onesb"] = np.ones((1, 128), ml_dtypes.bfloat16)
    d["onesf"] = np.ones((1, 128), f32)
    d["identb"] = np.eye(128, dtype=f32).astype(ml_dtypes.bfloat16)
    ep = np.zeros((128, 4), f32)
    ep[:, 0] = 1e-5; ep[:, 1] = 1e-6; ep[:, 2] = 1.0
    d["epsc"] = ep
    return d


def build(nb, ntt):
    NTOT = nb * ntt
    NT = NTOT * 128
    NTB = NT // 512
    assert NT % 512 == 0
    TT = ntt * 128

    nc = bass.Bass("TRN2", target_bir_lowering=False)
    inp = {}
    for name, shape, dt in [
        ("x", [nb, TT, D], F32), ("wqt", [D, D], F32), ("wkt", [D, D], F32),
        ("wvt", [D, D], F32), ("wot", [D, D], F32), ("wgat", [D, 8], F32),
        ("bga", [1, 8], F32), ("qk", [1, 1], F32), ("wgf", [D, 8], F32),
        ("bgf", [1, 8], F32), ("w1p", [NE, NJC, 128, 1024], F32),
        ("w2n", [NE, HF, D], F32), ("b1c", [NE, 128, NJC], F32), ("b2r", [NE, D], F32),
        ("cmask", [128, 128], F32), ("pow2", [128, 8], F32),
        ("onesb", [1, 128], BF16), ("onesf", [1, 128], F32), ("identb", [128, 128], BF16), ("epsc", [128, 4], F32),
    ]:
        inp[name] = nc.dram_tensor(name, shape, dt, kind="ExternalInput").ap()
    out_d = nc.dram_tensor("out", [nb, TT, D], F32, kind="ExternalOutput").ap()
    r_d = nc.dram_tensor("r_scr", [NT, D], F32, kind="Internal").ap()
    gam_d = nc.dram_tensor("gam_scr", [NTOT, 16], F32, kind="Internal").ap()
    w1bf_d = nc.dram_tensor("w1bf", [NE, NJC, 128, 1024], BF16, kind="Internal").ap()
    w2bf_d = nc.dram_tensor("w2bf", [NE, HF, D], BF16, kind="Internal").ap()

    rec = Rec()
    T = Eng("T"); V = Eng("V", auto_self=True); S = Eng("S", auto_self=True); SY = Eng("SY"); G = Eng("G")
    rec.engs = {e.name: e for e in (T, V, S, SY, G)}

    with ExitStack() as ctx:
        En = ctx.enter_context

        xnT = En(nc.sbuf_tensor("xnT", [128, 8 * NT], BF16))
        oh = En(nc.sbuf_tensor("oh", [128, NTOT * 8], F32))
        WT_holder = []
        wgat_sb = En(nc.sbuf_tensor("wgat_sb", [128, 64], BF16))
        wgf_sb = En(nc.sbuf_tensor("wgf_sb", [128, 64], BF16))
        bga_sb = En(nc.sbuf_tensor("bga_sb", [1, 8], F32))
        bgf_sb = En(nc.sbuf_tensor("bgf_sb", [1, 8], F32))
        cmask_sb = En(nc.sbuf_tensor("cmask_sb", [128, 128], F32))
        pow2_sb = En(nc.sbuf_tensor("pow2_sb", [128, 8], F32))
        onesb_sb = En(nc.sbuf_tensor("onesb_sb", [1, 128], BF16))
        onesf_sb = En(nc.sbuf_tensor("onesf_sb", [1, 128], F32))
        ident_sb = En(nc.sbuf_tensor("ident_sb", [128, 128], BF16))
        qk_sb = En(nc.sbuf_tensor("qk_sb", [1, 1], F32))
        lnqk_sb = En(nc.sbuf_tensor("lnqk_sb", [128, 1], F32))
        b2sb = En(nc.sbuf_tensor("b2sb", [1, NE * D], BF16))
        b2all = En(nc.sbuf_tensor("b2all", [8, D], BF16))
        ohb = En(nc.sbuf_tensor("ohb", [128, 8], BF16))
        ohT_sb = En(nc.sbuf_tensor("ohT_sb", [8, 128], BF16))
        epsc_sb = En(nc.sbuf_tensor("epsc_sb", [128, 4], F32))
        b1sb = En(nc.sbuf_tensor("b1sb", [128, NE * NJC], F32))

        dkeys = (["ld", "ldg"] + ["pc%d" % e for e in range(NE)]
                 + ["xs0", "xs1", "xs2", "rst0", "rst1"]
                 + ["w1s%d" % s for s in range(8)] + ["w2s%d" % s for s in range(8)]
                 + ["fr0", "fr1", "fst0", "fst1", "gw", "gr"])
        for k in dkeys:
            rec.dsems[k] = En(nc.semaphore("ds_" + k))
        sems = {}
        for e in (T, V, S, SY, G):
            sems[e.name] = En(nc.semaphore("sem_" + e.name))
        rec.engsems = sems

        def reset(emit):
            pairs = [(T, nc.tensor), (V, nc.vector), (S, nc.scalar), (SY, nc.sync), (G, nc.gpsimd)]
            for e, real in pairs:
                e.eng = real if emit else None
                e.sem = sems[e.name] if emit else None
                e.n = 0
            rec.dn = {}

        def phase0():
            SY.dma(cmask_sb[:, :], inp["cmask"], "ld", rec)
            SY.dma(pow2_sb[:, :], inp["pow2"], "ld", rec)
            SY.dma(onesf_sb[:, :], inp["onesf"], "ld", rec)
            SY.dma(onesb_sb[:, :], inp["onesb"], "ld", rec)
            SY.dma(qk_sb[:, :], inp["qk"], "ld", rec)
            SY.dma(bga_sb[:, :], inp["bga"], "ld", rec)
            SY.dma(bgf_sb[:, :], inp["bgf"], "ld", rec)
            SY.dma(b1sb[:, :].rearrange("p (e c) -> p e c", e=NE),
                   inp["b1c"].rearrange("e p c -> p e c"), "ld", rec)
            SY.dma(epsc_sb[:, :], inp["epsc"], "ld", rec, mark="ld_done")
            for w in ("wqt", "wkt", "wvt", "wot"):
                G.dma(WT[w][:, :].rearrange("p (dc o) -> p dc o", dc=8),
                      inp[w].rearrange("(dc p) o -> p dc o", p=128), "ldg", rec)
            G.dma(wgat_sb[:, :].rearrange("p (dc h) -> p dc h", dc=8),
                  inp["wgat"].rearrange("(dc p) h -> p dc h", p=128), "ldg", rec)
            G.dma(wgf_sb[:, :].rearrange("p (dc h) -> p dc h", dc=8),
                  inp["wgf"].rearrange("(dc p) h -> p dc h", p=128), "ldg", rec)
            G.dma(b2sb[:, :].rearrange("o (e d) -> o e d", e=NE),
                  inp["b2r"].unsqueeze(0), "ldg", rec)
            G.dma(b2all[:, :], inp["b2r"], "ldg", rec)
            G.dma(ident_sb[:, :], inp["identb"], "ldg", rec, mark="ldg_done")

        pc_chunks = []
        for e in range(NE):
            for q4 in range(4):
                pc_chunks.append((w1bf_d[e, q4 * 8:(q4 + 1) * 8],
                                  inp["w1p"][e, q4 * 8:(q4 + 1) * 8], "pc%d" % e))
            for q4 in range(4):
                pc_chunks.append((w2bf_d[e, q4 * 1024:(q4 + 1) * 1024],
                                  inp["w2n"][e, q4 * 1024:(q4 + 1) * 1024], "pc%d" % e))

        attn_ctx = ExitStack()
        An = attn_ctx.enter_context
        WT = {w: An(nc.sbuf_tensor(w + "_sb", [128, 8 * 1024], BF16))
              for w in ("wqt", "wkt", "wvt", "wot")}
        x_sb = An(nc.sbuf_tensor("x_sb", [128, 3 * 1024], F32))
        junk = An(nc.sbuf_tensor("junk", [128, 1024], BF16))
        small = An(nc.sbuf_tensor("small", [128, 16], F32))
        sqq = An(nc.sbuf_tensor("sqq", [128, 1024], BF16))
        ssq = An(nc.sbuf_tensor("ssq", [128, 8], F32))
        rstdq = An(nc.sbuf_tensor("rstdq", [128, 8], F32))
        rstdk = An(nc.sbuf_tensor("rstdk", [128, 8], F32))
        qhat = An(nc.sbuf_tensor("qhat", [128, 1024], BF16))
        khat = An(nc.sbuf_tensor("khat", [128, 1024], BF16))
        vhat = An(nc.sbuf_tensor("vhat", [128, 1024], BF16))
        qT_r = An(nc.sbuf_tensor("qT_r", [128, 1024], BF16))
        kT_r = An(nc.sbuf_tensor("kT_r", [128, 1024], BF16))
        aT_r = An(nc.sbuf_tensor("aT_r", [128, 1024], BF16))
        yT_r = An(nc.sbuf_tensor("yT_r", [128, 1024], BF16))
        sc_m = An(nc.sbuf_tensor("sc_m", [128, 1024], F32))
        attb = An(nc.sbuf_tensor("attb", [128, 1024], BF16))
        state = An(nc.sbuf_tensor("state", [128, 2048], F32))
        st_bf = An(nc.sbuf_tensor("st_bf", [128, 2048], BF16))
        r_ring = An(nc.sbuf_tensor("r_ring", [128, 2 * 1024], F32))
        lngb = An(nc.sbuf_tensor("lngb", [8, 128], F32))
        zabs = An(nc.sbuf_tensor("zabs", [8, 128], F32))
        lseg = An(nc.sbuf_tensor("lseg", [8, 2], F32))
        gseg = An(nc.sbuf_tensor("gseg", [8, 2], F32))
        g128 = An(nc.sbuf_tensor("g128", [128, 16], F32))
        mx8 = An(nc.sbuf_tensor("mx8", [128, 8], F32))
        sm8 = An(nc.sbuf_tensor("sm8", [128, 8], F32))
        eq8 = An(nc.sbuf_tensor("eq8", [128, 8], F32))

        pq = En(nc.psum_tensor("pq", [128, 1024], F32))
        ptr = En(nc.psum_tensor("ptr", [128, 1024], BF16))
        psc = En(nc.psum_tensor("psc", [128, 1024], F32))
        py = En(nc.psum_tensor("py", [128, 1024], F32))
        pmsc = En(nc.psum_tensor("pmsc", [128, 512], F32))

        LD_DONE = ["ld_done"]

        def attn_phase():
            for e in (T, V, S):
                e.wait("ld_done", rec)
                e.wait("ldg_done", rec)
            S.op("activation", small[0:1, 15:16], qk_sb[0:1, 0:1], AF.Ln, mark="lnqk_s", rec=rec)
            T.wait("lnqk_s", rec)
            T.mm(pmsc[:, 320:321], onesf_sb[0:1, :], small[0:1, 15:16], start=True, stop=True,
                 mark="lnqk_t", rec=rec)
            S.wait("lnqk_t", rec)
            S.op("copy", lnqk_sb[:, 0:1], pmsc[:, 320:321], mark="lnqk128", rec=rec)

            def sy_emit(k):
                if k >= 1:
                    jp = order[k - 1]
                    SY.wait(("gex", jp), rec)
                    SY.dma(gam_d[jp:jp + 1, :].rearrange("o (h c) -> (o h) c", h=8),
                           gseg[:, :], "gw", rec)
                    if k >= 2:
                        SY.wait(("so", order[k - 2]), rec)
                    SY.wait_dma("gw", 16 * k, rec)
                    src = bass.AP(gam_d.tensor, jp * 16, [[0, 128], [2, 8], [1, 2]])
                    g128v = g128[:, :].rearrange("p (h c) -> p h c", h=8)
                    rec.dn["gr"] = rec.dn.get("gr", 0) + 16
                    if SY.eng is not None:
                        SY.eng.dma_start(g128v, src).then_inc(rec.dsems["gr"], 16)
                    rec.marks[("g128", jp)] = ("DMA:gr", rec.dn["gr"])
                if k == 0:
                    for kj in range(min(2, NTOT)):
                        jj = order[kj]
                        SY.dma(x_sb[:, (kj % 3) * 1024:(kj % 3 + 1) * 1024],
                               inp["x"][jj // ntt, (jj % ntt) * 128:(jj % ntt + 1) * 128, :],
                               "xs%d" % (kj % 3), rec)
                kj = k + 2
                if kj < NTOT:
                    jj = order[kj]
                    if k >= 1:
                        SY.wait(("Radd", order[k - 1]), rec)
                    SY.dma(x_sb[:, (kj % 3) * 1024:(kj % 3 + 1) * 1024],
                           inp["x"][jj // ntt, (jj % ntt) * 128:(jj % ntt + 1) * 128, :],
                           "xs%d" % (kj % 3), rec)
                if k >= 1:
                    jp = order[k - 1]
                    SY.wait(("Radd", jp), rec)
                    SY.dma(r_d[jp * 128:(jp + 1) * 128, :],
                           r_ring[:, ((k - 1) % 2) * 1024:((k - 1) % 2 + 1) * 1024],
                           "rst%d" % ((k - 1) % 2), rec)
                    c0 = (k - 1) * len(pc_chunks) // NTOT
                    c1 = k * len(pc_chunks) // NTOT
                    if c1 > c0:
                        G.wait(("Radd", jp), rec)
                        for dst, src, key in pc_chunks[c0:c1]:
                            G.dma(dst, src, key, rec)

            def ln_emit(k, i, ip):
                sl = k % 3
                xs = x_sb[:, sl * 1024:(sl + 1) * 1024]
                V.wait_dma("xs%d" % sl, 16 * (k // 3 + 1), rec)
                V.op("reduce_sum", small[:, 0:1], xs, axis=AX.X, mark=("s1", i), rec=rec)
                S.wait_dma("xs%d" % sl, 16 * (k // 3 + 1), rec)
                if ip is not None:
                    S.wait(("trxn", ip), rec)
                S.op("activation", junk[:, :], xs, AF.Square, accum_out=small[:, 1:2],
                     mark=("s2", i), rec=rec)
                V.op("tensor_scalar_mul", small[:, 2:3], small[:, 0:1], 1.0 / D)
                V.wait(("s2", i), rec)
                V.op("tensor_scalar_mul", small[:, 3:4], small[:, 1:2], 1.0 / D)
                V.op("tensor_mul", small[:, 4:5], small[:, 2:3], small[:, 2:3])
                V.op("tensor_sub", small[:, 5:6], small[:, 3:4], small[:, 4:5], mark=("var", i), rec=rec)
                S.wait(("var", i), rec)
                S.op("activation", small[:, 6:7], small[:, 5:6], AF.Ln, bias=epsc_sb[:, 0:1])
                S.op("activation", small[:, 7:8], small[:, 6:7], AF.Exp, scale=-0.5,
                     mark=("rstd", i), rec=rec)
                V.wait(("rstd", i), rec)
                V.op("tensor_scalar", junk[:, :], xs, small[:, 2:3], small[:, 7:8],
                     ALU.subtract, ALU.mult, mark=("xn", i), rec=rec)
                T.wait(("xn", i), rec)
                if ip is not None:
                    T.wait(("kTe", ip), rec)
                for dc in range(8):
                    T.op("transpose", ptr[:, dc * 128:(dc + 1) * 128],
                         junk[:, dc * 128:(dc + 1) * 128], ident_sb[:, :],
                         mark=("trxn", i) if dc == 7 else None, rec=rec)
                S.wait(("trxn", i), rec)
                S.op("copy",
                     xnT[:, :].rearrange("p (dc t) -> p dc t", dc=8)[:, :, i * 128:i * 128 + 128],
                     ptr[:, :].rearrange("p (dc t) -> p dc t", dc=8),
                     mark=("xnev", i), rec=rec)

            def smax_emit(i, ip, inx):
                V.wait(("zg", i), rec)
                V.op("tensor_scalar_min", lngb[:, :], pmsc[0:8, 0:128], 0.0, mark=("zmin", i), rec=rec)
                V.wait(("sc", i), rec)
                if ip is not None:
                    V.wait(("exp", ip), rec)
                V.op("scalar_tensor_tensor", sc_m[:, :].rearrange("p (s x) -> p s x", s=8),
                     psc[:, :].rearrange("p (s x) -> p s x", s=8), RT2,
                     cmask_sb[:, :].unsqueeze(1).broadcast_to((128, 8, 128)),
                     ALU.mult, ALU.add, mark=("msk", i), rec=rec)
                V.op("reduce_max", mx8[:, :], sc_m[:, :].rearrange("p (s x) -> p s x", s=8),
                     axis=AX.X)
                V.op("scalar_tensor_tensor", sc_m[:, :].rearrange("p (s x) -> p s x", s=8),
                     sc_m[:, :].rearrange("p (s x) -> p s x", s=8), 1.0,
                     mx8[:, :].unsqueeze(2).broadcast_to((128, 8, 128)),
                     ALU.mult, ALU.subtract, mark=("sub", i), rec=rec)
                S.wait(("sub", i), rec)
                if ip is not None:
                    S.wait(("att", ip), rec)
                S.op("activation", sc_m[:, :], sc_m[:, :], AF.Exp, mark=("exp", i), rec=rec)
                V.wait(("exp", i), rec)
                V.op("reduce_sum", sm8[:, :], sc_m[:, :].rearrange("p (s x) -> p s x", s=8),
                     axis=AX.X)
                V.op("reciprocal", sm8[:, :], sm8[:, :])
                if ip is not None:
                    V.wait(("tra", ip), rec)
                V.op("tensor_tensor", attb[:, :].rearrange("p (s x) -> p s x", s=8),
                     sc_m[:, :].rearrange("p (s x) -> p s x", s=8),
                     sm8[:, :].unsqueeze(2).broadcast_to((128, 8, 128)),
                     ALU.mult, mark=("att", i), rec=rec)
                S.wait(("zg", i), rec)
                S.op("activation", zabs[:, :], pmsc[0:8, 0:128], AF.Abs)
                S.op("activation", zabs[:, :], zabs[:, :], AF.Exp, scale=-1.0)
                S.op("activation", zabs[:, :], zabs[:, :], AF.Ln, bias=epsc_sb[0:8, 2:3],
                     mark=("lnl", i), rec=rec)
                V.wait(("lnl", i), rec)
                V.op("tensor_sub", lngb[:, :], lngb[:, :], zabs[:, :])
                V.op("reduce_sum", lseg[:, :], lngb[:, :].rearrange("h (c x) -> h c x", c=2),
                     axis=AX.X, mark=("lns", i), rec=rec)
                S.wait(("lns", i), rec)
                S.wait_dma("gw", max(0, rec.dn.get("gw", 0) - 16), rec)
                S.op("activation", gseg[:, :], lseg[:, :], AF.Exp, mark=("gex", i), rec=rec)
                T.wait(("att", i), rec)
                if inx is not None:
                    T.wait(("xnev", inx), rec)
                for h in range(8):
                    T.op("transpose", ptr[:, h * 128:(h + 1) * 128],
                         attb[:, h * 128:(h + 1) * 128], ident_sb[:, :],
                         mark=("tra", i) if h == 7 else None, rec=rec)
                S.wait(("tra", i), rec)
                if ip is not None:
                    S.wait(("yin", ip), rec)
                S.op("copy", aT_r[:, :], ptr[:, :], mark=("aTe", i), rec=rec)
                V.wait(("gf", i), rec)
                V.op("reduce_max", mx8[:, 0:1], pmsc[:, 128:136], axis=AX.X)
                V.op("tensor_scalar", eq8[:, :], pmsc[:, 128:136], mx8[:, 0:1], None,
                     ALU.is_equal, mark=("ohq", i), rec=rec)
                V.op("tensor_mul", eq8[:, :], eq8[:, :], pow2_sb[:, :])
                V.op("reduce_max", mx8[:, 1:2], eq8[:, :], axis=AX.X)
                V.op("tensor_scalar", oh[:, i * 8:(i + 1) * 8], eq8[:, :], mx8[:, 1:2], None,
                     ALU.is_equal, mark=("ohe", i), rec=rec)
                S.wait(("ohe", i), rec)
                if ip is not None:
                    S.wait(("ohT", ip), rec)
                S.op("copy", ohb[:, :], oh[:, i * 8:(i + 1) * 8], mark=("ohb", i), rec=rec)
                T.wait(("ohb", i), rec)
                if ip is not None:
                    T.wait(("ohTe", ip), rec)
                T.op("transpose", pmsc[0:8, 384:448].bitcast(BF16), ohb[:, :],
                     ident_sb[:, :], mark=("ohT", i), rec=rec)
                S.wait(("ohT", i), rec)
                if ip is not None:
                    S.wait(("wo", ip), rec)
                S.op("copy", ohT_sb[:, :], pmsc[0:8, 384:448].bitcast(BF16), mark=("ohTe", i), rec=rec)

            def scan_emit(i, ip, rp, sl):
                b, tt = divmod(i, ntt)
                stv = state[:, b * 1024:(b + 1) * 1024]
                stb = st_bf[:, b * 1024:(b + 1) * 1024]
                xs = x_sb[:, sl * 1024:(sl + 1) * 1024]
                if tt == 0:
                    V.op("memset", stv, 0.0)
                    V.op("memset", stb, 0.0, mark=("st0", b), rec=rec)
                T.wait(("aTe", i), rec)
                T.wait(("vev", i), rec)
                if ip is not None:
                    T.wait(("yTe", ip), rec)
                if tt == 0:
                    T.wait(("st0", b), rec)
                else:
                    T.wait(("sbo", i - 1), rec)
                for h in range(8):
                    T.mm(py[:, h * 128:(h + 1) * 128], vhat[:, h * 128:(h + 1) * 128],
                         aT_r[:, h * 128:(h + 1) * 128], start=True, stop=True,
                         skip_group_check=True,
                         mark=("yin", i) if h == 7 else None, rec=rec)
                for h in range(8):
                    T.mm(py[:, h * 128:h * 128 + 64], stb[:, h * 128:(h + 1) * 128],
                         qT_r[:, h * 128:h * 128 + 64], start=False, stop=True,
                         skip_group_check=True, rec=rec)
                T.wait(("msk", i), rec)
                for h in range(8):
                    T.mm(psc[:, h * 128:(h + 1) * 128], khat[0:64, h * 128:(h + 1) * 128],
                         vhat[0:64, h * 128:(h + 1) * 128], start=True, stop=True,
                         mark=("kve", i) if h == 7 else None, rec=rec)
                V.wait(("kve", i), rec)
                V.wait(("g128", i), rec)
                if tt == 0:
                    V.wait(("st0", b), rec)
                for hh in range(2):
                    hs = slice(hh * 512, (hh + 1) * 512)
                    V.op("tensor_tensor", stv[:, hs].rearrange("p (s x) -> p s x", s=4),
                         stv[:, hs].rearrange("p (s x) -> p s x", s=4),
                         g128[:, :].rearrange("p (h c) -> p h c", h=8)[:, hh * 4:hh * 4 + 4, 0:1]
                         .broadcast_to((128, 4, 128)),
                         ALU.mult)
                    V.op("tensor_add", stv[:, hs], stv[:, hs], psc[:, hs],
                         mark=("seA", i) if hh == 0 else ("se", i), rec=rec)
                    V.op("tensor_copy", stb[:, hs], stv[:, hs],
                         mark=("sbeA", i) if hh == 0 else ("sbe", i), rec=rec)
                T.wait(("sbeA", i), rec)
                for h in range(4):
                    T.mm(py[:, h * 128 + 64:h * 128 + 128], stb[:, h * 128:(h + 1) * 128],
                         qT_r[:, h * 128 + 64:h * 128 + 128], start=False, stop=True,
                         skip_group_check=True, rec=rec)
                T.wait(("seA", i), rec)
                for h in range(4):
                    T.mm(psc[:, h * 128:(h + 1) * 128], khat[64:128, h * 128:(h + 1) * 128],
                         vhat[64:128, h * 128:(h + 1) * 128], start=True, stop=True, rec=rec)
                T.wait(("sbe", i), rec)
                for h in range(4, 8):
                    T.mm(py[:, h * 128 + 64:h * 128 + 128], stb[:, h * 128:(h + 1) * 128],
                         qT_r[:, h * 128 + 64:h * 128 + 128], start=False, stop=True,
                         skip_group_check=True,
                         mark=("yco", i) if h == 7 else None, rec=rec)
                T.wait(("se", i), rec)
                for h in range(4, 8):
                    T.mm(psc[:, h * 128:(h + 1) * 128], khat[64:128, h * 128:(h + 1) * 128],
                         vhat[64:128, h * 128:(h + 1) * 128], start=True, stop=True,
                         mark=("kvo", i) if h == 7 else None, rec=rec)
                V.wait(("kvo", i), rec)
                V.op("tensor_tensor", stv.rearrange("p (s x) -> p s x", s=8),
                     stv.rearrange("p (s x) -> p s x", s=8),
                     g128[:, :].rearrange("p (h c) -> p h c", h=8)[:, :, 1:2].broadcast_to((128, 8, 128)),
                     ALU.mult)
                V.op("tensor_add", stv, stv, psc[:, :], mark=("so", i), rec=rec)
                V.op("tensor_copy", stb, stv, mark=("sbo", i), rec=rec)
                S.wait(("yco", i), rec)
                if ip is not None:
                    S.wait(("wo", ip), rec)
                S.op("copy", yT_r[:, :], py[:, :], mark=("yTe", i), rec=rec)
                T.wait(("yTe", i), rec)
                T.wait(("qh", i), rec)
                T.wait(("ohTe", i), rec)
                for half in range(2):
                    for ec in range(8):
                        T.mm(pq[:, half * 512:(half + 1) * 512], yT_r[:, ec * 128:(ec + 1) * 128],
                             WT["wot"][:, ec * 1024 + half * 512:ec * 1024 + (half + 1) * 512],
                             start=(ec == 0), stop=False, rec=rec)
                    T.mm(pq[:, half * 512:(half + 1) * 512], ohT_sb[:, :],
                         b2all[:, half * 512:(half + 1) * 512],
                         start=False, stop=True,
                         mark=("wo", i) if half == 1 else ("woA", i), rec=rec)
                prev_rst = rec.dn.get("rst%d" % rp, 0) - 16
                for half in range(2):
                    V.wait(("wo", i) if half == 1 else ("woA", i), rec)
                    if half == 0 and prev_rst > 0:
                        V.wait_dma("rst%d" % rp, prev_rst, rec)
                    hs = slice(half * 512, (half + 1) * 512)
                    V.op("tensor_add", r_ring[:, rp * 1024 + half * 512:rp * 1024 + (half + 1) * 512],
                         pq[:, hs], xs[:, hs],
                         mark=("Radd", i) if half == 1 else ("RaddA", i), rec=rec)

            def proj_emit(i, ip):
                t0 = i * 128
                T.wait(("xnev", i), rec)
                for half in range(2):
                    if ip is not None:
                        T.wait(("Radd", ip) if half == 1 else ("RaddA", ip), rec)
                    for dc in range(8):
                        T.mm(pq[:, half * 512:(half + 1) * 512],
                             xnT[:, dc * NT + t0:dc * NT + t0 + 128],
                             WT["wqt"][:, dc * 1024 + half * 512:dc * 1024 + (half + 1) * 512],
                             start=(dc == 0), stop=(dc == 7),
                             mark=("qp", i) if (half == 1 and dc == 7) else None, rec=rec)
                if ip is not None:
                    T.wait(("zmin", ip), rec)
                    T.wait(("lnl", ip), rec)
                T.mm(pmsc[0:8, 0:128], bga_sb[0:1, :], onesf_sb[0:1, :], start=True, stop=False)
                for dc in range(8):
                    T.mm(pmsc[0:8, 0:128], wgat_sb[:, dc * 8:(dc + 1) * 8],
                         xnT[:, dc * NT + t0:dc * NT + t0 + 128],
                         start=False, stop=(dc == 7), mark=("zg", i) if dc == 7 else None, rec=rec)
                if ip is not None:
                    T.wait(("ohq", ip), rec)
                T.mm(pmsc[:, 128:136], onesf_sb[0:1, :], bgf_sb[0:1, :], start=True, stop=False)
                for dc in range(8):
                    T.mm(pmsc[:, 128:136], xnT[:, dc * NT + t0:dc * NT + t0 + 128],
                         wgf_sb[:, dc * 8:(dc + 1) * 8],
                         start=False, stop=(dc == 7), mark=("gf", i) if dc == 7 else None, rec=rec)
                S.wait(("qp", i), rec)
                if ip is not None:
                    S.wait(("kss", ip), rec)
                S.op("activation", sqq[:, 0:512], pq[:, 0:512], AF.Square)
                S.op("activation", sqq[:, 512:1024], pq[:, 512:1024], AF.Square,
                     mark=("qsq", i), rec=rec)
                V.wait(("qsq", i), rec)
                V.op("reduce_sum", ssq[:, :], sqq[:, :].rearrange("p (s x) -> p s x", s=8),
                     axis=AX.X, mark=("qss", i), rec=rec)
                S.wait(("qss", i), rec)
                S.op("activation", rstdq[:, :], ssq[:, :], AF.Ln, scale=1.0 / 128, bias=epsc_sb[:, 1:2])
                S.wait("lnqk128", rec)
                S.op("activation", rstdq[:, :], rstdq[:, :], AF.Exp, scale=-0.5,
                     bias=lnqk_sb[:, 0:1], mark=("rq", i), rec=rec)
                V.wait(("rq", i), rec)
                if ip is not None:
                    V.wait(("trq", ip), rec)
                V.op("tensor_tensor", qhat[:, :].rearrange("p (s x) -> p s x", s=8),
                     pq[:, :].rearrange("p (s x) -> p s x", s=8),
                     rstdq[:, :].unsqueeze(2).broadcast_to((128, 8, 128)),
                     ALU.mult, mark=("qh", i), rec=rec)
                if ip is not None:
                    T.wait(("so", ip), rec)
                for half in range(2):
                    for dc in range(8):
                        T.mm(psc[:, half * 512:(half + 1) * 512],
                             xnT[:, dc * NT + t0:dc * NT + t0 + 128],
                             WT["wkt"][:, dc * 1024 + half * 512:dc * 1024 + (half + 1) * 512],
                             start=(dc == 0), stop=(dc == 7),
                             mark=("kp", i) if (half == 1 and dc == 7) else None, rec=rec)
                S.wait(("kp", i), rec)
                S.wait(("qss", i), rec)
                S.op("activation", sqq[:, 0:512], psc[:, 0:512], AF.Square)
                S.op("activation", sqq[:, 512:1024], psc[:, 512:1024], AF.Square,
                     mark=("ksq", i), rec=rec)
                V.wait(("ksq", i), rec)
                V.op("reduce_sum", ssq[:, :], sqq[:, :].rearrange("p (s x) -> p s x", s=8),
                     axis=AX.X, mark=("kss", i), rec=rec)
                S.wait(("kss", i), rec)
                S.op("activation", rstdk[:, :], ssq[:, :], AF.Ln, scale=1.0 / 128, bias=epsc_sb[:, 1:2])
                S.op("activation", rstdk[:, :], rstdk[:, :], AF.Exp, scale=-0.5,
                     mark=("rk", i), rec=rec)
                V.wait(("rk", i), rec)
                if ip is not None:
                    V.wait(("trk", ip), rec)
                    V.wait(("kvo", ip), rec)
                V.op("tensor_tensor", khat[:, :].rearrange("p (s x) -> p s x", s=8),
                     psc[:, :].rearrange("p (s x) -> p s x", s=8),
                     rstdk[:, :].unsqueeze(2).broadcast_to((128, 8, 128)),
                     ALU.mult, mark=("kh", i), rec=rec)
                if ip is not None:
                    T.wait(("yTe", ip), rec)
                for half in range(2):
                    for dc in range(8):
                        T.mm(py[:, half * 512:(half + 1) * 512],
                             xnT[:, dc * NT + t0:dc * NT + t0 + 128],
                             WT["wvt"][:, dc * 1024 + half * 512:dc * 1024 + (half + 1) * 512],
                             start=(dc == 0), stop=(dc == 7),
                             mark=("vp", i) if (half == 1 and dc == 7) else None, rec=rec)
                S.wait(("vp", i), rec)
                if ip is not None:
                    S.wait(("kvo", ip), rec)
                S.op("copy", vhat[:, :], py[:, :], mark=("vev", i), rec=rec)
                T.wait(("qh", i), rec)
                if ip is not None:
                    T.wait(("aTe", ip), rec)
                for dc in range(8):
                    T.op("transpose", ptr[:, dc * 128:(dc + 1) * 128],
                         qhat[:, dc * 128:(dc + 1) * 128], ident_sb[:, :],
                         mark=("trq", i) if dc == 7 else None, rec=rec)
                S.wait(("trq", i), rec)
                if ip is not None:
                    S.wait(("yco", ip), rec)
                S.op("copy", qT_r[:, :], ptr[:, :], mark=("qTe", i), rec=rec)
                T.wait(("qTe", i), rec)
                T.wait(("kh", i), rec)
                for dc in range(8):
                    T.op("transpose", ptr[:, dc * 128:(dc + 1) * 128],
                         khat[:, dc * 128:(dc + 1) * 128], ident_sb[:, :],
                         mark=("trk", i) if dc == 7 else None, rec=rec)
                S.wait(("trk", i), rec)
                if ip is not None:
                    S.wait(("sc", ip), rec)
                S.op("copy", kT_r[:, :], ptr[:, :], mark=("kTe", i), rec=rec)
                T.wait(("kTe", i), rec)
                T.wait(("kh", i), rec)
                for h in range(8):
                    T.mm(psc[:, h * 128:(h + 1) * 128], qT_r[:, h * 128:(h + 1) * 128],
                         kT_r[:, h * 128:(h + 1) * 128], start=True, stop=True,
                         mark=("sc", i) if h == 7 else None, rec=rec)

            order = [(k % nb) * ntt + (k // nb) for k in range(NTOT)]
            for k in range(NTOT):
                i = order[k]
                ip = order[k - 1] if k >= 1 else None
                ipp = order[k - 2] if k >= 2 else None
                sy_emit(k)
                ln_emit(k, i, ip)
                if k >= 1:
                    smax_emit(ip, ipp, i)
                    scan_emit(ip, ipp, (k - 1) % 2, (k - 1) % 3)
                proj_emit(i, ip)
            k = NTOT
            iL = order[NTOT - 1]
            sy_emit(k)
            smax_emit(iL, order[NTOT - 2] if NTOT >= 2 else None, None)
            scan_emit(iL, order[NTOT - 2] if NTOT >= 2 else None, (NTOT - 1) % 2, (NTOT - 1) % 3)
            SY.wait_dma("rst0", rec.dn["rst0"], rec)
            SY.wait_dma("rst1", rec.dn.get("rst1", 0), rec)

        attn_ctx.close()
        w1r = En(nc.sbuf_tensor("w1r", [128, 8 * 1024], BF16))
        w2r = En(nc.sbuf_tensor("w2r", [128, 8 * 1024], BF16))
        hT = En(nc.sbuf_tensor("hT", [128, NJC * 512], BF16))
        outb = [En(nc.sbuf_tensor("outb%d" % p, [128, 4 * 1024], F32)) for p in range(2)]
        psf = [pq[:, 0:512], pq[:, 512:1024], psc[:, 0:512], psc[:, 512:1024],
               py[:, 0:512], py[:, 512:1024], pmsc[:, 0:512], ptr[:, :].bitcast(F32)]

        def ffn_phase():
            U = NTB * NE
            NL = NTOT - 1
            for m in [("wo", NL), ("ohe", NL), ("yTe", NL), ("sbo", NL), ("g128", NL), ("Radd", NL)]:
                SY.wait(m, rec)
            for u in range(U):
                tb, e = divmod(u, NE)
                for jc in range(NJC):
                    g = u * NJC + jc
                    if tb == 0 and jc == 0:
                        SY.wait_dma("pc%d" % e, 128, rec)
                    if g >= 8:
                        SY.wait(("f1", g - 8), rec)
                    SY.dma(w1r[:, (g % 8) * 1024:(g % 8 + 1) * 1024], w1bf_d[e, jc],
                           "w1s%d" % (g % 8), rec)
            for m in [("wo", NL), ("ohe", NL), ("yTe", NL), ("sbo", NL), ("g128", NL), ("Radd", NL)]:
                G.wait(m, rec)
            G.wait_dma("rst0", rec.dn["rst0"], rec)
            G.wait_dma("rst1", rec.dn["rst1"], rec)
            for u in range(U):
                tb, e = divmod(u, NE)
                for jc in range(NJC):
                    g = u * NJC + jc
                    if tb == 0 and jc == 0:
                        G.wait_dma("pc%d" % e, 128, rec)
                    if g >= 8:
                        G.wait(("f2", g - 8), rec)
                    G.dma(w2r[:, (g % 8) * 1024:(g % 8 + 1) * 1024],
                          w2bf_d[e, jc * 128:(jc + 1) * 128, :], "w2s%d" % (g % 8), rec)
            for u in range(U):
                tb, e = divmod(u, NE)
                if e == 0:
                    T.wait(("xnev", min((tb + 1) * 4, NTOT) - 1), rec)
                for jc in range(NJC):
                    g = u * NJC + jc
                    T.wait_dma("w1s%d" % (g % 8), 16 * (g // 8 + 1), rec)
                    bank = jc % 2
                    if jc >= 2:
                        T.wait(("fs", g - 2), rec)
                    elif u > 0:
                        T.wait(("fv", (u - 1) * 8 + bank), rec)
                    for dc in range(8):
                        T.mm(psf[bank],
                             w1r[:, (g % 8) * 1024 + dc * 128:(g % 8) * 1024 + (dc + 1) * 128],
                             xnT[:, dc * NT + tb * 512:dc * NT + (tb + 1) * 512],
                             start=(dc == 0), stop=(dc == 7),
                             mark=("f1", g) if dc == 7 else None, rec=rec)
                for jc in range(NJC):
                    g = u * NJC + jc
                    T.wait_dma("w2s%d" % (g % 8), 16 * (g // 8 + 1), rec)
                    T.wait(("fs", g), rec)
                    if jc == 0:
                        if u > 0:
                            T.wait(("fv", u * 8 - 1), rec)
                        ots = [2, 3, 4, 5, 6, 7, 0, 1]
                    else:
                        ots = list(range(8))
                    for ot in ots:
                        if jc == 0 and ot == 0:
                            T.wait(("fs", u * NJC + NJC - 1), rec)
                        tsub, dsl = divmod(ot, 2)
                        if ot == ots[-1]:
                            mk = ("f2", g)
                        elif jc == NJC - 1:
                            mk = ("f2o", u * 8 + ot)
                        else:
                            mk = None
                        T.mm(psf[ot],
                             hT[:, jc * 512 + tsub * 128:jc * 512 + (tsub + 1) * 128],
                             w2r[:, (g % 8) * 1024 + dsl * 512:(g % 8) * 1024 + (dsl + 1) * 512],
                             start=(jc == 0), stop=(jc == NJC - 1),
                             mark=mk, rec=rec)
            S.wait(("ohe", NL), rec)
            S.wait(("wo", NL), rec)
            S.wait(("sbo", NL), rec)
            S.wait(("Radd", NL), rec)
            for u in range(U):
                tb, e = divmod(u, NE)
                if e == 0:
                    if tb >= 2:
                        S.wait_dma("fst%d" % (tb % 2), 16 * ((tb - 2) // 2 + 1), rec)
                    need = [0, 0]
                    for g in range(tb * 4, tb * 4 + 4):
                        kg = (g % ntt) * nb + (g // ntt)
                        need[kg % 2] = max(need[kg % 2], kg // 2 + 1)
                    for par in range(2):
                        if need[par]:
                            S.wait_dma("rst%d" % par, 16 * need[par], rec)
                    S.dma(outb[tb % 2][:, :].rearrange("p (ts d) -> p ts d", ts=4),
                          r_d.rearrange("(a ts p) d -> a p ts d", p=128, ts=4)[tb],
                          "fr%d" % (tb % 2), rec)
                if e == min(1, NE - 1) and tb > 0:
                    S.wait(("fvU", tb - 1), rec)
                    S.dma(out_flat.rearrange("(a ts p) d -> a p ts d", p=128, ts=4)[tb - 1],
                          outb[(tb - 1) % 2][:, :].rearrange("p (ts d) -> p ts d", ts=4),
                          "fst%d" % ((tb - 1) % 2), rec)
                for jc in range(NJC):
                    g = u * NJC + jc
                    S.wait(("f1", g), rec)
                    if u > 0:
                        S.wait(("f2", (u - 1) * NJC + jc), rec)
                    S.op("activation", hT[:, jc * 512:(jc + 1) * 512], psf[jc % 2],
                         AF.Relu, bias=b1sb[:, e * NJC + jc:e * NJC + jc + 1],
                         mark=("fs", g), rec=rec)
            S.wait(("fvU", NTB - 1), rec)
            S.dma(out_flat.rearrange("(a ts p) d -> a p ts d", p=128, ts=4)[NTB - 1],
                  outb[(NTB - 1) % 2][:, :].rearrange("p (ts d) -> p ts d", ts=4),
                  "fst%d" % ((NTB - 1) % 2), rec)
            for par in range(2):
                tot = len([t_ for t_ in range(NTB) if t_ % 2 == par])
                if tot:
                    S.wait_dma("fst%d" % par, 16 * tot, rec)
            for u in range(U):
                tb, e = divmod(u, NE)
                ob = outb[tb % 2]
                if e == 0:
                    V.wait_dma("fr%d" % (tb % 2), 16 * (tb // 2 + 1), rec)
                for ot in range(8):
                    V.wait(("f2o", u * 8 + ot) if ot < 7 else ("f2", (u + 1) * NJC - 1), rec)
                    tsub, dsl = divmod(ot, 2)
                    obs = ob[:, tsub * 1024 + dsl * 512:tsub * 1024 + (dsl + 1) * 512]
                    V.op("scalar_tensor_tensor", obs, psf[ot],
                         oh[:, (tb * 4 + tsub) * 8 + e:(tb * 4 + tsub) * 8 + e + 1],
                         obs, ALU.mult, ALU.add,
                         mark=("fv", u * 8 + ot), rec=rec)
                if e == NE - 1:
                    rec.marks[("fvU", tb)] = ("V", V.n)

        out_flat = out_d.rearrange("b t d -> (b t) d")

        def run_program():
            phase0()
            attn_phase()
            ffn_phase()

        reset(False)
        run_program()

        reset(True)
        live = {e.name: (e.eng, e.sem) for e in (T, V, S, SY, G)}

        def emit_for(target):
            for e in (T, V, S, SY, G):
                if e.name == target:
                    e.eng, e.sem = live[e.name]
                else:
                    e.eng, e.sem = None, None
                e.n = 0
            rec.dn = {}
            run_program()

        with nc.Block("mk", no_gpsimd_drain=True) as blk:
            blk.sync(lambda _e: emit_for("SY"))
            blk.gpsimd(lambda _e: emit_for("G"))
            blk.tensor(lambda _e: emit_for("T"))
            blk.scalar(lambda _e: emit_for("S"))
            blk.vector(lambda _e: emit_for("V"))

    return nc


_PROG_CACHE = {}


def kernel(**inputs):
    from concourse.bass_utils import run_bass_kernel_spmd

    nb, ntt = 2, 16
    if "prog" not in _PROG_CACHE:
        _PROG_CACHE["prog"] = build(nb, ntt)
    nc = _PROG_CACHE["prog"]
    in_maps = [host_prep(inputs, nb, ntt, core * nb) for core in range(8)]
    res = run_bass_kernel_spmd(nc, in_maps, core_ids=list(range(8)))
    return np.concatenate([np.asarray(r["out"]) for r in res.results], axis=0)
